# version 01
# baseline (speedup 1.0000x reference)
import numpy as np
import ml_dtypes
from contextlib import ExitStack, contextmanager

import concourse.bass as bass
import concourse.bacc as bacc
import concourse.mybir as mybir
from concourse.bass_utils import run_bass_kernel_spmd
from concourse.ordered_engine_set import OrderedEngineSet

F32 = mybir.dt.float32
BF16 = mybir.dt.bfloat16
AF = mybir.ActivationFunctionType
ALU = mybir.AluOpType
AX = mybir.AxisListType
ts = bass.ts

B, S, E, Hf = 4, 2048, 4096, 11008
NKV, DH = 8, 128
T, P = 1024, 512
DC = E // 128
HT = 88
HPAD = HT * 128
W3C = 2816
NU12, NU3, NUIO = 22, 16, 10
EPS = 1e-5
SCL = float(np.float32(1.0) / np.sqrt(np.float32(DH)))
NEG = -1e30
SZ12 = 128 * E
SZ3 = 128 * W3C

_NF4 = np.array([-1.0, -0.6961928009986877, -0.5250730514526367,
                 -0.39491748809814453, -0.28444138169288635, -0.18477343022823334,
                 -0.09105003625154495, 0.0, 0.07958029955625534, 0.16093020141124725,
                 0.24611230194568634, 0.33791524171829224, 0.44070982933044434,
                 0.5626170039176941, 0.7229568362236023, 1.0], dtype=np.float32)
_BOUNDS = ((_NF4[:-1] + _NF4[1:]) * np.float32(0.5)).astype(np.float32)
_DELTA = (_NF4[1:] - _NF4[:-1]).astype(np.float32)

SUBLAYERS = ("ffn1", "attn", "ffn2")

KSLOT = [11 * (t % 8) + (t // 8) for t in range(HT)]


def dsl(ap2d, ncols, r0, nr, c0, ncp):
    return bass.AP(ap2d.tensor, ap2d.offset + r0 * ncols + c0, [[ncols, nr], [1, ncp]])


def d3(ap3, i, w, n=1):
    return bass.AP(ap3.tensor, ap3.offset + i * (128 * w), [[w, 128], [128 * w, n], [1, w]])


class KB:
    def __init__(self, sublayers, stop=None):
        self.stop = stop
        self.sub = tuple(sublayers)
        nc = self.nc = bacc.Bacc("TRN2", target_bir_lowering=False)
        self.es = ExitStack()
        D = self.D = {}

        def dram(n, shp, dt, kind=None):
            D[n] = (nc.dram_tensor(n, shp, dt, kind=kind) if kind
                    else nc.dram_tensor(n, shp, dt)).ap()

        dram("x_fm", [E, T], F32, "ExternalInput")
        dram("w12c", [NU12, 128, E], F32, "ExternalInput")
        dram("w3c", [NU3, 128, E], F32, "ExternalInput")
        dram("wioc", [NUIO, 128, E], F32, "ExternalInput")
        dram("cosq", [64, T], F32, "ExternalInput")
        dram("sinq", [64, T], F32, "ExternalInput")
        dram("masks", [16, 128, P], BF16, "ExternalInput")
        dram("ident", [128, 128], BF16, "ExternalInput")
        dram("onesb", [128, 1], BF16, "ExternalInput")
        dram("onesf", [1, 128], F32, "ExternalInput")
        dram("nfb", [128, 15], F32, "ExternalInput")
        dram("out_fm", [E, T], F32, "ExternalOutput")
        dram("own12", [NU12, 128, E], BF16)
        dram("own3", [NU3, 128, W3C], BF16)
        dram("ownio", [NUIO, 128, E], BF16)
        dram("g12", [8 * NU12, 128, E], BF16)
        dram("g3", [8 * NU3, 128, W3C], BF16)
        dram("gio", [8 * NUIO, 128, E], BF16)
        dram("kown", [NKV, 128, T], BF16)
        dram("vown", [NKV, T, DH], BF16)
        dram("gk", [2, NKV, 128, T], BF16)
        dram("gv", [2, NKV, T, DH], BF16)
        dram("x1_fm", [E, T], F32)
        dram("x2_fm", [E, T], F32)
        dram("dbg1", [128, 45056], BF16, "ExternalOutput")

        sbt = lambda n, s, d: self.es.enter_context(nc.sbuf_tensor(n, s, d))
        pst = lambda n, s, d: self.es.enter_context(nc.psum_tensor(n, s, d))
        self.AW = 77824
        self.arena = sbt("arena", [128, self.AW], BF16)
        self.tsg = sbt("tsg", [128, 2 * P], F32)
        self.xsq = sbt("xsq", [128, 2 * P], BF16)
        self.hv = sbt("hv", [128, 2 * P], BF16)
        self.xr = sbt("xr", [128, 2 * P], F32)
        self.xo = sbt("xo", [128, 2 * P], F32)
        self.rows = sbt("rows", [1, 4 * P], F32)
        self.abc = sbt("abc", [128, P], F32)
        self.bbc = sbt("bbc", [128, P], F32)
        self.ptb = sbt("ptb", [128, 2 * P], BF16)
        self.lrow = sbt("lrow", [1, 2 * P], F32)
        self.rlb = sbt("rlb", [128, P], F32)
        self.vt = sbt("vt", [128, 2 * P], BF16)
        self.ktx = sbt("ktx", [128, 2 * P], BF16)
        self.vp = sbt("vp", [128, 2 * P], BF16)
        self.id_sb = sbt("id_sb", [128, 128], BF16)
        self.onesb_sb = sbt("onesb_sb", [128, 1], BF16)
        self.onesf_sb = sbt("onesf_sb", [1, 128], F32)
        self.cs_sb = sbt("cs_sb", [64, 2 * T], F32)
        self.nfb_sb = sbt("nfb_sb", [128, 15], F32)
        self.pb = [pst(f"pb{i}", [128, 512], F32) for i in range(6)]
        self.pt = [pst(f"pt{i}", [128, 1024], BF16) for i in range(2)]

        names = ["p_ld", "p_vv", "p_sg", "p_ac", "p_vd", "p_tp", "p_ve", "p_st",
                 "cc1", "cc2", "cc3", "cck", "ccv",
                 "f_lx", "f_sq", "f_st1", "f_sml", "f_bc", "f_xln",
                 "f_lw", "f_gu", "f_sg", "f_h", "f_hp", "f_l3", "f_d",
                 "f_xr", "f_x1", "f_so",
                 "a_qk", "a_rp", "a_vv", "a_kst", "a_lkv", "a_mk", "g_ld", "a_ml", "o_xr", "o_so",
                 "a_sc", "a_ex", "a_pv", "a_l", "a_no", "o_mm", "o_res"]
        self.S = {n: self.es.enter_context(nc.semaphore(n)) for n in names}
        self.C = {n: 0 for n in names}
        self._mf = 0
        self._streams = {}

    def abf(self, off_b, n, col=0, npart=128):
        return bass.AP(self.arena[:, :].tensor, off_b // 2 + col,
                       [[self.AW, npart], [1, n]])

    def af32(self, off_b, n):
        t = self.arena[:, :].bitcast(F32)
        return bass.AP(t.tensor, t.offset + off_b // 4, [[self.AW // 2, 128], [1, n]])

    def W(self, eng, sn, val):
        if isinstance(val, int) and val <= 0:
            return
        eng.wait_ge(self.S[sn], val)

    @contextmanager
    def my_fori(self, eng, niter):
        self._mf += 1
        name = f"mf{self._mf}_{eng.engine}"
        nc = self.nc
        with eng.register(name) as r:
            eng.reg_mov(r, 0)
            engines = OrderedEngineSet([eng.engine])
            nc.br(f"{name}_s", engines=engines)
            with nc.body(f"{name}_s", valid_engines=engines):
                yield
                eng.reg_add(r, r, 1)
                nc.br_lt(r, niter, on_true=f"{name}_s", on_false=f"{name}_e",
                         engines=engines)
            nc.switch_bb(f"{name}_e")

    def rs(self, eng, name):
        key = (str(eng.engine), name)
        st = self._streams.get(key)
        if st is None:
            reg = eng.alloc_register(f"rs_{name}")
            eng.reg_mov(reg, 0)
            st = {"reg": reg, "cur": 0}
            self._streams[key] = st
        return st

    def rs_to(self, eng, st, val):
        d = val - st["cur"]
        if d:
            eng.reg_add(st["reg"], st["reg"], d)
        st["cur"] = val

    def rs_add(self, eng, st, d):
        if d:
            eng.reg_add(st["reg"], st["reg"], d)
        st["cur"] += d

    @contextmanager
    def rloop(self, eng, niter, streams):
        entry = [st["cur"] for st in streams]
        with self.my_fori(eng, niter):
            yield
        for st, e0 in zip(streams, entry):
            st["cur"] = e0 + niter * (st["cur"] - e0)

    def build(self):
        nc = self.nc
        with nc.Block() as block:
            C0 = dict(self.C)

            def run(E_, eng):
                self.C = dict(C0)
                self._streams = {}
                self.emit_all(E_, eng)

            @block.sync
            def _(e):
                run("sync", e)

            @block.tensor
            def _(e):
                run("tensor", e)

            @block.scalar
            def _(e):
                run("scalar", e)

            @block.vector
            def _(e):
                run("vector", e)

            @block.gpsimd
            def _(e):
                run("gpsimd", e)
        self.es.close()
        nc.finalize()
        return nc

    def emit_all(self, E_, eng):
        D = self.D
        S_ = self.S
        self.phase_prep(E_, eng)
        self.phase_gather(E_, eng)
        if self.stop in ("prep", "prep5"):
            if E_ == "sync":
                eng.wait_ge(S_["cc3"], 1)
                eng.dma_start(out=dsl(D["dbg1"], 45056, 0, 128, 0, 4096),
                              in_=D["g12"][23]).then_inc(S_["f_so"], 16)
                eng.dma_start(out=dsl(D["dbg1"], 45056, 0, 128, 4096, W3C),
                              in_=D["g3"][17]).then_inc(S_["f_so"], 16)
                eng.dma_start(out=dsl(D["dbg1"], 45056, 0, 128, 4096 + W3C, 4096),
                              in_=D["gio"][12]).then_inc(S_["f_so"], 16)
                for di, gu in enumerate([5, 9, 97, 4]):
                    eng.dma_start(out=dsl(D["dbg1"], 45056, 0, 128,
                                          4096 + W3C + 4096 + di * W3C, W3C),
                                  in_=D["g3"][gu]).then_inc(S_["f_so"], 16)
                for di, ou in enumerate([5, 1, 0]):
                    eng.dma_start(out=dsl(D["dbg1"], 45056, 0, 128,
                                          4096 + W3C + 4096 + 4 * W3C + di * W3C, W3C),
                                  in_=D["own3"][ou]).then_inc(S_["f_so"], 16)
            self.C["f_so"] += 160
            if E_ == "gpsimd":
                eng.wait_ge(S_["f_so"], 160)
            return
        last = self.sub[-1]
        dst = D["out_fm"] if last == "ffn1" else D["x1_fm"]
        if self.stop in ("ln", "gate"):
            xg = [("f_gu", 0), ("f_d", 0), ("o_res", 0)]
            self.phase_ln(E_, eng, D["x_fm"], 0, [])
            if self.stop == "gate":
                self.phase_ffn_gate_only(E_, eng)
                if E_ == "sync":
                    eng.wait_ge(S_["f_hp"], self.C["f_hp"])
                    eng.dma_start(out=D["dbg1"],
                                  in_=self.abf(0, HT * P)).then_inc(S_["f_so"], 16)
            else:
                if E_ == "sync":
                    eng.wait_ge(S_["f_xln"], self.C["f_xln"])
                    eng.dma_start(out=dsl(D["dbg1"], 45056, 0, 128, 0, DC * P),
                                  in_=self.abf(90112, DC * P)).then_inc(S_["f_so"], 16)
            self.C["f_so"] += 16
            if E_ == "gpsimd":
                eng.wait_ge(S_["f_so"], self.C["f_so"])
            return
        self.sublayer_ffn(E_, eng, D["x_fm"], dst)
        if "attn" in self.sub:
            src = dst
            dst = D["out_fm"] if last == "attn" else D["x2_fm"]
            self.phase_attn(E_, eng, src, dst)
            if "ffn2" in self.sub:
                self.sublayer_ffn(E_, eng, dst, D["out_fm"])
        if E_ == "gpsimd":
            eng.wait_ge(self.S["f_so"], self.C["f_so"])
            self.W(eng, "o_so", self.C["o_so"])

    def prep_pipeline(self, E_, eng, src, dst, W_, nunits, dequant, ldsem="p_ld",
                      store_w=None):
        C, S_ = self.C, self.S
        NB, NT_ = W_ // 64, W_ // 128
        NG = (NT_ + 7) // 8
        b = {k: C[k] for k in ("p_vv", "p_sg", "p_ac", "p_vd",
                               "p_tp", "p_ve", "p_st")}
        b[ldsem] = C[ldsem]
        wraw = [self.af32(0, W_), self.af32(16384, W_)]
        sd = [self.abf(32768, W_), self.abf(40960, W_)]
        accd = self.af32(49152, W_)
        dqb = [self.abf(65536, W_), self.abf(73728, W_)]
        slab = [self.abf(81920, W_), self.abf(90112, W_)]
        amax = [self.af32(98304, 64), self.af32(98816, 64)]
        rec = [self.af32(99328, 64), self.af32(99840, 64)]
        eqb = self.af32(100352, 64)

        for u in range(nunits):
            par = u % 2
            wv = wraw[par]
            if (self.stop == "prep5" and W_ == W3C and u == 6
                    and E_ == "vector"):
                eng.wait_ge(S_["a_mk"], 112)
            if (self.stop == "prep5" and W_ == W3C and u == 7
                    and E_ == "sync"):
                eng.wait_ge(S_["a_mk"], 112)
            if E_ == "sync" and dequant:
                self.W(eng, "p_ac", b["p_ac"] + 15 * (u - 1))
                eng.dma_start(out=wraw[par], in_=src[u]).then_inc(S_[ldsem], 16)
            if dequant and E_ == "vector":
                eng.wait_ge(S_[ldsem], b[ldsem] + 16 * (u + 1))
                w3v = wv.rearrange("p (b j) -> p b j", j=64)
                amv = bass.AP(amax[par].tensor, amax[par].offset,
                              [list(amax[par].ap[0]), [1, NB], [1, 1]])
                eng.reduce_max(amv, w3v, axis=AX.X, apply_absolute_value=True)
                eng.tensor_scalar(eqb, amax[par], 0.0, None, ALU.is_equal)
                eng.tensor_tensor(eqb, eqb, amax[par], ALU.add)
                eng.reciprocal(rec[par], eqb)
                if W_ == 4096:
                    rb = rec[par]
                    rb3 = bass.AP(rb.tensor, rb.offset,
                                  [list(rb.ap[0]), list(rb.ap[1]), [0, 64]])
                    eng.tensor_tensor(w3v, w3v, rb3, ALU.mult).then_inc(S_["p_vv"], 1)
                else:
                    mm = None
                    for bb in range(NB):
                        blkap = bass.AP(wv.tensor, wv.offset + 64 * bb,
                                        [list(wv.ap[0]), [1, 64]])
                        mm = eng.tensor_scalar(blkap, blkap,
                                               bass.AP(rec[par].tensor,
                                                       rec[par].offset + bb,
                                                       [list(rec[par].ap[0]), [1, 1]]),
                                               None, ALU.mult)
                    mm.then_inc(S_["p_vv"], 1)
            if dequant and E_ == "scalar":
                eng.wait_ge(S_["p_vv"], b["p_vv"] + (u + 1))
                for j in range(15):
                    self.W(eng, "p_ac", b["p_ac"] + 15 * u + j - 1)
                    eng.activation(sd[j % 2], wv, AF.Sign,
                                   bias=self.nfb_sb[:, j:j + 1]).then_inc(S_["p_sg"], 1)
            if dequant and E_ == "vector":
                for j in range(15):
                    eng.wait_ge(S_["p_sg"], b["p_sg"] + 15 * u + j + 1)
                    dj = float(np.float32(_DELTA[j]) * np.float32(0.5))
                    if j == 0:
                        eng.tensor_scalar(accd, sd[0], dj, None,
                                          ALU.mult).then_inc(S_["p_ac"], 1)
                    else:
                        eng.scalar_tensor_tensor(accd, sd[j % 2], dj, accd, ALU.mult,
                                                 ALU.add).then_inc(S_["p_ac"], 1)
                am = amax[par]
                self.W(eng, "p_tp", b["p_tp"] + NG * (u - 1))
                if W_ == 4096:
                    am3 = bass.AP(am.tensor, am.offset,
                                  [list(am.ap[0]), list(am.ap[1]), [0, 64]])
                    eng.tensor_tensor(dqb[par].rearrange("p (b j) -> p b j", j=64),
                                      accd.rearrange("p (b j) -> p b j", j=64),
                                      am3, ALU.mult).then_inc(S_["p_vd"], 1)
                else:
                    mm = None
                    for bb in range(NB):
                        oap = bass.AP(dqb[par].tensor, dqb[par].offset + 64 * bb,
                                      [list(dqb[par].ap[0]), [1, 64]])
                        iap = bass.AP(accd.tensor, accd.offset + 64 * bb,
                                      [list(accd.ap[0]), [1, 64]])
                        mm = eng.tensor_scalar(oap, iap,
                                               bass.AP(am.tensor, am.offset + bb,
                                                       [list(am.ap[0]), [1, 1]]),
                                               None, ALU.mult)
                    mm.then_inc(S_["p_vd"], 1)
            if E_ == "tensor":
                if dequant:
                    eng.wait_ge(S_["p_vd"], b["p_vd"] + (u + 1))
                else:
                    eng.wait_ge(S_[ldsem], b[ldsem] + 16 * (u + 1))
                for gb in range(NG):
                    npc = min(8, NT_ - 8 * gb)
                    gi = NG * u + gb
                    self.W(eng, "p_ve", b["p_ve"] + gi - 1)
                    mm = None
                    for k in range(npc):
                        mm = eng.transpose(self.pt[gi % 2][:, ts(k, 128)],
                                           dqb[par][:, ts(8 * gb + k, 128)],
                                           self.id_sb[:, :])
                    mm.then_inc(S_["p_tp"], 1)
            if E_ == "vector":
                for gb in range(NG):
                    npc = min(8, NT_ - 8 * gb)
                    gi = NG * u + gb
                    eng.wait_ge(S_["p_tp"], b["p_tp"] + gi + 1)
                    if gb == 0:
                        self.W(eng, "p_st", b["p_st"] + 16 * (u - 1))
                    eng.tensor_copy(slab[par][:, 1024 * gb:1024 * gb + 128 * npc],
                                    self.pt[gi % 2][:, 0:128 * npc]).then_inc(S_["p_ve"], 1)
            if E_ == "sync":
                eng.wait_ge(S_["p_ve"], b["p_ve"] + NG * (u + 1))
                sw = store_w or W_
                eng.dma_start(out=dst[u],
                              in_=self.abf(81920 + 8192 * par, sw)
                              ).then_inc(S_["p_st"], 16)

        C[ldsem] = b[ldsem] + 16 * nunits
        C["p_st"] = b["p_st"] + 16 * nunits
        C["p_tp"] = b["p_tp"] + NG * nunits
        C["p_ve"] = b["p_ve"] + NG * nunits
        if dequant:
            C["p_vv"] = b["p_vv"] + nunits
            C["p_sg"] = b["p_sg"] + 15 * nunits
            C["p_ac"] = b["p_ac"] + 15 * nunits
            C["p_vd"] = b["p_vd"] + nunits

    def phase_prep(self, E_, eng):
        C, S_, D = self.C, self.S, self.D
        if E_ == "sync":
            eng.dma_start(out=self.id_sb[:, :], in_=D["ident"]).then_inc(S_["p_ld"], 16)
            eng.dma_start(out=self.onesb_sb[:, :], in_=D["onesb"]).then_inc(S_["p_ld"], 16)
            eng.dma_start(out=self.onesf_sb[:, :], in_=D["onesf"]).then_inc(S_["p_ld"], 16)
            eng.dma_start(out=self.cs_sb[:, 0:T], in_=D["cosq"]).then_inc(S_["p_ld"], 16)
            eng.dma_start(out=self.cs_sb[:, T:2 * T], in_=D["sinq"]).then_inc(S_["p_ld"], 16)
            eng.dma_start(out=self.nfb_sb[:, :], in_=D["nfb"]).then_inc(S_["p_ld"], 16)
        if E_ == "tensor":
            eng.wait_ge(S_["p_ld"], 16)
        C["p_ld"] = 6 * 16
        self.prep_pipeline(E_, eng, D["w12c"], D["own12"], 4096, NU12, True)
        self.prep_pipeline(E_, eng, D["w3c"], D["own3"], 4096, NU3, True,
                           store_w=W3C)
        if self.stop == "prep5" and E_ == "gpsimd":
            eng.wait_ge(S_["p_sg"], 420)
            eng.wait_ge(S_["p_vd"], 28)
            eng.wait_ge(S_["p_ve"], 106)
            dumps = [(98816, 128), (59392, 512), (78848, 256), (95232, 256),
                     (26624, 512), (37888, 256), (46080, 256)]
            col = 31000
            for off_b, n in dumps:
                eng.dma_start(out=dsl(D["dbg1"], 45056, 0, 128, col, n),
                              in_=self.abf(off_b, n)).then_inc(S_["a_mk"], 16)
                col += n
        b3_tp = C["p_tp"]
        if E_ == "gpsimd":
            dqb = [self.abf(65536, 4096), self.abf(73728, 4096)]
            for u in range(NUIO):
                eng.wait_ge(S_["p_tp"], b3_tp + max(0, 4 * (u - 1)))
                eng.dma_start(out=dqb[u % 2], in_=D["wioc"][u]).then_inc(S_["g_ld"], 16)
        self.prep_pipeline(E_, eng, D["wioc"], D["ownio"], 4096, NUIO, False, ldsem="g_ld")

    def phase_gather(self, E_, eng):
        C, S_, D = self.C, self.S, self.D
        if E_ == "gpsimd":
            eng.wait_ge(S_["p_st"], C["p_st"])
            eng.collective_compute("AllGather", ALU.bypass, [list(range(8))],
                                   [D["own12"].opt()], [D["g12"].opt()]).then_inc(S_["cc1"], 1)
            eng.collective_compute("AllGather", ALU.bypass, [list(range(8))],
                                   [D["own3"].opt()], [D["g3"].opt()]).then_inc(S_["cc2"], 1)
            eng.collective_compute("AllGather", ALU.bypass, [list(range(8))],
                                   [D["ownio"].opt()], [D["gio"].opt()]).then_inc(S_["cc3"], 1)
        C["cc1"] = C["cc2"] = C["cc3"] = 1

    def phase_ln(self, E_, eng, xsrc, p, xg):
        C, S_ = self.C, self.S
        b = {k: C[k] for k in ("f_lx", "f_sq", "f_st1", "f_sml", "f_bc", "f_xln")}
        xln = lambda col: self.abf(90112, P, col)
        mrow = self.rows[0:1, 0:P]
        vrow = self.rows[0:1, P:2 * P]
        arow = self.rows[0:1, 2 * P:3 * P]
        brow = self.rows[0:1, 3 * P:4 * P]
        if E_ == "gpsimd":
            for gg in xg:
                self.W(eng, gg[0], gg[1])
            self.W(eng, "o_so", C["o_so"])
            src = bass.AP(xsrc.tensor, xsrc.offset + p * P,
                          [[T, 128], [128 * T, DC], [1, P]])
            dst = bass.AP(self.arena[:, :].tensor, 90112 // 2,
                          [[self.AW, 128], [P, DC], [1, P]])
            eng.dma_start(out=dst, in_=src).then_inc(S_["f_lx"], 16)
        if E_ == "scalar":
            eng.wait_ge(S_["f_lx"], b["f_lx"] + 16)
            for i in range(DC):
                self.W(eng, "f_st1", b["f_st1"] + i - 1)
                eng.activation(self.xsq[:, ts(i % 2, P)], xln(i * P),
                               AF.Square).then_inc(S_["f_sq"], 1)
        if E_ == "tensor":
            for gg in xg:
                self.W(eng, gg[0], gg[1])
            self.W(eng, "f_xln", b["f_xln"])
            eng.wait_ge(S_["f_lx"], b["f_lx"] + 16)
            for i in range(DC):
                eng.wait_ge(S_["f_sq"], b["f_sq"] + i + 1)
                eng.matmul(self.pb[4][0:1, :], self.onesb_sb[:, :], xln(i * P),
                           start=(i == 0), stop=(i == DC - 1))
                eng.matmul(self.pb[5][0:1, :], self.onesb_sb[:, :],
                           self.xsq[:, ts(i % 2, P)],
                           start=(i == 0), stop=(i == DC - 1)).then_inc(S_["f_st1"], 1)
        if E_ == "vector":
            eng.wait_ge(S_["f_st1"], b["f_st1"] + DC)
            eng.tensor_scalar(mrow, self.pb[4][0:1, :], 1.0 / E, None, ALU.mult)
            eng.tensor_scalar(vrow, self.pb[5][0:1, :], 1.0 / E, None, ALU.mult)
            eng.tensor_tensor(brow, mrow, mrow, ALU.mult)
            eng.tensor_tensor(vrow, vrow, brow, ALU.subtract)
            eng.tensor_scalar(vrow, vrow, EPS, None, ALU.add)
            eng.reciprocal(arow, vrow).then_inc(S_["f_sml"], 1)
        if E_ == "scalar":
            eng.wait_ge(S_["f_sml"], b["f_sml"] + 1)
            eng.activation(arow, arow, AF.Sqrt).then_inc(S_["f_sml"], 1)
        if E_ == "vector":
            eng.wait_ge(S_["f_sml"], b["f_sml"] + 2)
            eng.scalar_tensor_tensor(brow, mrow, -1.0, arow, ALU.mult,
                                     ALU.mult).then_inc(S_["f_sml"], 1)
        if E_ == "tensor":
            eng.wait_ge(S_["f_sml"], b["f_sml"] + 3)
            eng.matmul(self.pb[4][:, :], self.onesf_sb[:, :], arow, start=True, stop=True)
            eng.matmul(self.pb[5][:, :], self.onesf_sb[:, :], brow, start=True,
                       stop=True).then_inc(S_["f_bc"], 1)
        if E_ == "vector":
            eng.wait_ge(S_["f_bc"], b["f_bc"] + 1)
            eng.tensor_copy(self.abc[:, :], self.pb[4][:, :])
            eng.tensor_copy(self.bbc[:, :], self.pb[5][:, :])
            mm = None
            for i in range(DC):
                xc = xln(i * P)
                eng.tensor_tensor(xc, xc, self.abc[:, :], ALU.mult)
                mm = eng.tensor_tensor(xc, xc, self.bbc[:, :], ALU.add)
            mm.then_inc(S_["f_xln"], 1)
        C["f_lx"] = b["f_lx"] + 16
        C["f_sq"] = b["f_sq"] + DC
        C["f_st1"] = b["f_st1"] + DC
        C["f_sml"] = b["f_sml"] + 3
        C["f_bc"] = b["f_bc"] + 1
        C["f_xln"] = b["f_xln"] + 1

    def phase_ffn_gate_only(self, E_, eng):
        self.phase_ffn_pass(E_, eng, self.D["x_fm"], self.D["x1_fm"], 0, [],
                            gate_only=True)

    def phase_ffn_pass(self, E_, eng, xsrc, xdst, p, hg, gate_only=False):
        C, S_, D = self.C, self.S, self.D
        b = {k: C[k] for k in ("f_lw", "f_gu", "f_sg", "f_h", "f_hp", "f_l3",
                               "f_d", "f_xr", "f_x1", "f_so")}
        b_xln = C["f_xln"]
        b_d_prev = C["f_d"]
        b_x1_prev = C["f_x1"]
        xln = lambda col: self.abf(90112, P, col)
        wpair = [self.abf(122880, 8192), self.abf(139264, 8192)]
        slab3 = [self.abf(90112, HPAD), self.abf(112640, HPAD)]
        pb = self.pb

        def gate_slot(k, par):
            w1 = lambda d: self.abf(122880 + 16384 * par, 128, 128 * d)
            w2 = lambda d: self.abf(122880 + 16384 * par, 128, 4096 + 128 * d)
            if E_ == "tensor":
                st = self.rs(eng, "t_lw")
                st2 = self.rs(eng, "t_sg")
                st3 = self.rs(eng, "t_h")
                if isinstance(k, int):
                    eng.wait_ge(S_["f_lw"], b["f_lw"] + 16 * (k + 1))
                    self.W(eng, "f_sg", b["f_sg"] + k - 1)
                    self.W(eng, "f_h", b["f_h"] + k - 1)
                else:
                    eng.wait_ge(S_["f_lw"], st["reg"])
                    self.rs_add(eng, st, 16)
                    eng.wait_ge(S_["f_sg"], st2["reg"])
                    self.rs_add(eng, st2, 1)
                    eng.wait_ge(S_["f_h"], st3["reg"])
                    self.rs_add(eng, st3, 1)
                mm = None
                for d in range(DC):
                    mm = eng.matmul(pb[par][:, :], w1(d), xln(d * P),
                                    start=(d == 0), stop=(d == DC - 1))
                mm.then_inc(S_["f_gu"], 1)
                for d in range(DC):
                    mm = eng.matmul(pb[2 + par][:, :], w2(d), xln(d * P),
                                    start=(d == 0), stop=(d == DC - 1))
                mm.then_inc(S_["f_gu"], 1)
            if E_ == "scalar":
                st = self.rs(eng, "c_gu")
                sth = self.rs(eng, "c_h")
                if isinstance(k, int):
                    eng.wait_ge(S_["f_gu"], b["f_gu"] + 2 * k + 1)
                    self.W(eng, "f_h", b["f_h"] + k - 1)
                else:
                    eng.wait_ge(S_["f_gu"], st["reg"])
                    self.rs_add(eng, st, 2)
                    eng.wait_ge(S_["f_h"], sth["reg"])
                    self.rs_add(eng, sth, 1)
                eng.activation(self.tsg[:, ts(par, P)], pb[par][:, :],
                               AF.Silu).then_inc(S_["f_sg"], 1)
            if E_ == "vector":
                st = self.rs(eng, "v_sg")
                st2 = self.rs(eng, "v_gu")
                st3 = self.rs(eng, "v_hp")
                if isinstance(k, int):
                    eng.wait_ge(S_["f_sg"], b["f_sg"] + k + 1)
                    eng.wait_ge(S_["f_gu"], b["f_gu"] + 2 * k + 2)
                    self.W(eng, "f_hp", b["f_hp"] + 16 * (k // 2 - 1))
                else:
                    eng.wait_ge(S_["f_sg"], st["reg"])
                    self.rs_add(eng, st, 1)
                    eng.wait_ge(S_["f_gu"], st2["reg"])
                    self.rs_add(eng, st2, 2)
                    if par == 0:
                        eng.wait_ge(S_["f_hp"], st3["reg"])
                        self.rs_add(eng, st3, 16)
                eng.tensor_tensor(self.hv[:, ts(par, P)], self.tsg[:, ts(par, P)],
                                  pb[2 + par][:, :], ALU.mult).then_inc(S_["f_h"], 1)

        if E_ == "tensor":
            eng.wait_ge(S_["f_xln"], b_xln)
            self.W(eng, "f_x1", b_x1_prev)
            for gg in hg:
                self.W(eng, gg[0], gg[1])
        if E_ == "sync":
            eng.wait_ge(S_["cc1"], 1)
            self.W(eng, "f_d", b_d_prev)
            for gg in hg:
                self.W(eng, gg[0], gg[1])
            for k in range(HT):
                self.W(eng, "f_gu", b["f_gu"] + 2 * k - 1)
                eng.dma_start(out=wpair[k % 2], in_=d3(D["g12"], 2 * k, E, 2)
                              ).then_inc(S_["f_lw"], 16)
        if E_ == "gpsimd":
            st_h = self.rs(eng, "g_hd")
            st_w = self.rs(eng, "g_hw")
            for gg in hg:
                self.W(eng, gg[0], gg[1])
            self.W(eng, "f_d", b_d_prev)
            self.rs_to(eng, st_h, 0)
            self.rs_to(eng, st_w, b["f_h"] + 2)
            with self.rloop(eng, HT // 2, [st_h, st_w]):
                eng.wait_ge(S_["f_h"], st_w["reg"])
                self.rs_add(eng, st_w, 2)
                eng.dma_start(out=bass.AP(self.arena[:, :].tensor, st_h["reg"],
                                          [[self.AW, 128], [1, 1024]]),
                              in_=self.hv[:, 0:1024]).then_inc(S_["f_hp"], 16)
                self.rs_add(eng, st_h, 1024)
        if E_ == "tensor":
            for k in (0, 1):
                gate_slot(k, k)
            sts = [self.rs(eng, "t_lw"), self.rs(eng, "t_sg"), self.rs(eng, "t_h")]
            self.rs_to(eng, sts[0], b["f_lw"] + 48)
            self.rs_to(eng, sts[1], b["f_sg"] + 1)
            self.rs_to(eng, sts[2], b["f_h"] + 1)
            with self.rloop(eng, HT // 2 - 1, sts):
                gate_slot("x", 0)
                gate_slot("x", 1)
        if E_ in ("scalar", "vector"):
            for k in (0, 1):
                gate_slot(k, k)
            if E_ == "scalar":
                sts = [self.rs(eng, "c_gu"), self.rs(eng, "c_h")]
                self.rs_to(eng, sts[0], b["f_gu"] + 5)
                self.rs_to(eng, sts[1], b["f_h"] + 1)
            else:
                sts = [self.rs(eng, "v_sg"), self.rs(eng, "v_gu"), self.rs(eng, "v_hp")]
                self.rs_to(eng, sts[0], b["f_sg"] + 3)
                self.rs_to(eng, sts[1], b["f_gu"] + 6)
                self.rs_to(eng, sts[2], b["f_hp"] + 16)
            with self.rloop(eng, HT // 2 - 1, sts):
                gate_slot("x", 0)
                gate_slot("x", 1)
        C["f_lw"] = b["f_lw"] + 16 * HT
        C["f_gu"] = b["f_gu"] + 2 * HT
        C["f_sg"] = b["f_sg"] + HT
        C["f_h"] = b["f_h"] + HT
        C["f_hp"] = b["f_hp"] + 16 * (HT // 2)
        b2_gu, b2_hp = C["f_gu"], C["f_hp"]
        if gate_only:
            return

        hcol = lambda col: self.abf(0, P, col)

        def down_slot(i, q):
            if E_ == "tensor":
                st = self.rs(eng, "t_l3")
                st2 = self.rs(eng, "t_x1")
                if isinstance(i, int):
                    eng.wait_ge(S_["f_l3"], b["f_l3"] + 16 * (i + 1))
                    self.W(eng, "f_x1", b["f_x1"] + i - 1)
                else:
                    eng.wait_ge(S_["f_l3"], st["reg"])
                    self.rs_add(eng, st, 16)
                    eng.wait_ge(S_["f_x1"], st2["reg"])
                    self.rs_add(eng, st2, 1)
                mm = None
                for t in range(86):
                    mm = eng.matmul(pb[q][:, :], slab3[q][:, ts(t, 128)],
                                    hcol(KSLOT[t] * P), start=(t == 0), stop=(t == 85))
                mm.then_inc(S_["f_d"], 1)
            if E_ == "vector":
                st = self.rs(eng, "v_d")
                st2 = self.rs(eng, "v_xr")
                st3 = self.rs(eng, "v_so")
                if isinstance(i, int):
                    eng.wait_ge(S_["f_d"], b["f_d"] + i + 1)
                    eng.wait_ge(S_["f_xr"], b["f_xr"] + 16 * (i // 2 + 1))
                    self.W(eng, "f_so", b["f_so"] + 16 * (i // 2 - 1))
                else:
                    eng.wait_ge(S_["f_d"], st["reg"])
                    self.rs_add(eng, st, 1)
                    if q == 0:
                        eng.wait_ge(S_["f_xr"], st2["reg"])
                        self.rs_add(eng, st2, 16)
                        eng.wait_ge(S_["f_so"], st3["reg"])
                        self.rs_add(eng, st3, 16)
                eng.scalar_tensor_tensor(self.xo[:, ts(q, P)], pb[q][:, :], 0.5,
                                         self.xr[:, ts(q, P)], ALU.mult,
                                         ALU.add).then_inc(S_["f_x1"], 1)

        NB2 = DC // 2
        if E_ == "sync":
            eng.wait_ge(S_["cc2"], 1)
            eng.wait_ge(S_["f_gu"], b2_gu)
            for i in range(DC):
                self.W(eng, "f_d", b["f_d"] + i - 1)
                eng.dma_start(out=bass.AP(self.arena[:, :].tensor,
                                          (90112 + 22528 * (i % 2)) // 2,
                                          [[self.AW, 128], [W3C, 4], [1, W3C]]),
                              in_=d3(D["g3"], 4 * i, W3C, 4)).then_inc(S_["f_l3"], 16)
        if E_ == "gpsimd":
            st_x = self.rs(eng, "g_xr")
            st_w = self.rs(eng, "g_x1")
            st_s = self.rs(eng, "g_sod")
            self.W(eng, "o_res", C["o_res"])
            eng.dma_start(out=self.xr[:, 0:1024].rearrange("p (q j) -> p q j", j=P),
                          in_=bass.AP(xsrc.tensor, xsrc.offset + p * P,
                                      [[T, 128], [128 * T, 2], [1, P]])
                          ).then_inc(S_["f_xr"], 16)
            self.rs_to(eng, st_w, b["f_x1"] + 2)
            self.rs_to(eng, st_s, p * P)
            self.rs_to(eng, st_x, 256 * T + p * P)
            with self.rloop(eng, NB2 - 1, [st_w, st_s, st_x]):
                eng.wait_ge(S_["f_x1"], st_w["reg"])
                self.rs_add(eng, st_w, 2)
                eng.dma_start(out=bass.AP(xdst.tensor, st_s["reg"],
                                          [[T, 128], [128 * T, 2], [1, P]]),
                              in_=self.xo[:, 0:1024].rearrange("p (q j) -> p q j", j=P)
                              ).then_inc(S_["f_so"], 16)
                self.rs_add(eng, st_s, 256 * T)
                eng.dma_start(out=self.xr[:, 0:1024].rearrange("p (q j) -> p q j", j=P),
                              in_=bass.AP(xsrc.tensor, st_x["reg"],
                                          [[T, 128], [128 * T, 2], [1, P]])
                              ).then_inc(S_["f_xr"], 16)
                self.rs_add(eng, st_x, 256 * T)
            eng.wait_ge(S_["f_x1"], b["f_x1"] + 2 * NB2)
            eng.dma_start(out=bass.AP(xdst.tensor,
                                      xdst.offset + p * P + (NB2 - 1) * 256 * T,
                                      [[T, 128], [128 * T, 2], [1, P]]),
                          in_=self.xo[:, 0:1024].rearrange("p (q j) -> p q j", j=P)
                          ).then_inc(S_["f_so"], 16)
        if E_ in ("tensor", "vector"):
            if E_ == "tensor":
                eng.wait_ge(S_["f_hp"], b2_hp)
            for i in (0, 1):
                down_slot(i, i)
            if E_ == "tensor":
                sts = [self.rs(eng, "t_l3"), self.rs(eng, "t_x1")]
                self.rs_to(eng, sts[0], b["f_l3"] + 48)
                self.rs_to(eng, sts[1], b["f_x1"] + 1)
            else:
                sts = [self.rs(eng, "v_d"), self.rs(eng, "v_xr"), self.rs(eng, "v_so")]
                self.rs_to(eng, sts[0], b["f_d"] + 3)
                self.rs_to(eng, sts[1], b["f_xr"] + 32)
                self.rs_to(eng, sts[2], b["f_so"] + 16)
            with self.rloop(eng, NB2 - 1, sts):
                down_slot("x", 0)
                down_slot("x", 1)
        C["f_l3"] = b["f_l3"] + 16 * DC
        C["f_d"] = b["f_d"] + DC
        C["f_xr"] = b["f_xr"] + 16 * NB2
        C["f_x1"] = b["f_x1"] + DC
        C["f_so"] = b["f_so"] + 16 * NB2

    def sublayer_ffn(self, E_, eng, xsrc, xdst):
        for p in range(2):
            if self.C["o_mm"] and p == 0:
                hg = [("o_mm", self.C["o_mm"]), ("a_pv", self.C["a_pv"]),
                      ("o_res", self.C["o_res"])]
                xg = [("a_sc", self.C["a_sc"]), ("f_d", self.C["f_d"]),
                      ("f_gu", self.C["f_gu"]), ("o_res", self.C["o_res"])]
            else:
                hg = []
                xg = [("f_gu", self.C["f_gu"]), ("f_d", self.C["f_d"]),
                      ("o_res", self.C["o_res"])]
            self.phase_ln(E_, eng, xsrc, p, xg)
            self.phase_ffn_pass(E_, eng, xsrc, xdst, p, hg)

    def phase_attn(self, E_, eng, xsrc, xdst):
        C, S_, D = self.C, self.S, self.D
        b_fd = C["f_d"]
        for p in range(2):
            xg = ([("f_gu", C["f_gu"]), ("f_d", C["f_d"]), ("o_res", C["o_res"])]
                  if p == 0 else [("a_qk", C["a_qk"])])
            self.phase_ln(E_, eng, xsrc, p, xg)
            self.qkv_pass(E_, eng, p, b_fd)

        if E_ == "gpsimd":
            eng.wait_ge(S_["a_kst"], C["a_kst"])
            eng.collective_compute("AllGather", ALU.bypass, [[0, 1], [2, 3], [4, 5], [6, 7]],
                                   [D["kown"].opt()], [D["gk"].opt()]).then_inc(S_["cck"], 1)
            eng.collective_compute("AllGather", ALU.bypass, [[0, 1], [2, 3], [4, 5], [6, 7]],
                                   [D["vown"].opt()], [D["gv"].opt()]).then_inc(S_["ccv"], 1)
        C["cck"] = C["ccv"] = 1
        A2RL = [(0, 0), (1, 0), (1, 1), (0, 1)]
        b_lkv = C["a_lkv"]
        if E_ == "sync":
            eng.wait_ge(S_["cck"], 1)
            eng.wait_ge(S_["a_qk"], C["a_qk"])
            for kh in range(NKV):
                for a in range(4):
                    pr, lb = A2RL[a]
                    src = bass.AP(D["gk"].tensor,
                                  D["gk"].offset + (pr * NKV + kh) * 128 * T + lb * P,
                                  [[T, 128], [1, P]])
                    eng.dma_start(out=self.abf(90112, P, kh * 2048 + a * P),
                                  in_=src).then_inc(S_["a_lkv"], 16)
            eng.wait_ge(S_["ccv"], 1)
            for kh in range(NKV):
                for a in range(4):
                    pr, lb = A2RL[a]
                    src = bass.AP(D["gv"].tensor,
                                  D["gv"].offset + (pr * NKV + kh) * T * DH + lb * P * DH,
                                  [[DH, 128], [128 * DH, 4], [1, DH]])
                    dst = bass.AP(self.arena[:, :].tensor,
                                  122880 // 2 + (kh * 16 + 4 * a) * 128,
                                  [[self.AW, 128], [128, 4], [1, 128]])
                    eng.dma_start(out=dst, in_=src).then_inc(S_["a_lkv"], 16)
        C["a_lkv"] = b_lkv + 16 * 64

        for j in range(2):
            self.attn_chunk(E_, eng, j, xsrc, xdst)

    def qkv_pass(self, E_, eng, p, b_fd):
        C, S_, D = self.C, self.S, self.D
        pb = self.pb
        b = {k: C[k] for k in ("f_lw", "a_qk", "a_rp", "a_vv", "a_kst",
                               "p_tp", "p_ve")}
        xln = lambda col: self.abf(90112, P, col)
        qt = lambda col: self.abf(0, P, col)
        b_xln = C["f_xln"]
        wq = [self.abf(122880, 4096), self.abf(131072, 4096)]

        for ft in range(48):
            par = ft % 2
            cls = "q" if ft < 32 else ("k" if ft < 40 else "v")
            if E_ == "sync":
                if ft == 0:
                    eng.wait_ge(S_["cc3"], 1)
                    if p == 0:
                        self.W(eng, "f_x1", C["f_x1"])
                self.W(eng, "a_qk", b["a_qk"] + ft - 1)
                eng.dma_start(out=wq[par],
                              in_=D["gio"][10 * (ft // 6) + ft % 6]).then_inc(S_["f_lw"], 16)
            if E_ == "tensor":
                if ft == 0:
                    eng.wait_ge(S_["f_xln"], b_xln)
                eng.wait_ge(S_["f_lw"], b["f_lw"] + 16 * (ft + 1))
                pft = ft - 2
                if pft >= 0:
                    if pft < 40:
                        self.W(eng, "a_rp", b["a_rp"] + pft + 1)
                    else:
                        self.W(eng, "a_vv", b["a_vv"] + pft - 40 + 1)
                mm = None
                for d in range(DC):
                    mm = eng.matmul(pb[par][:, :], wq[par][:, ts(d, 128)], xln(d * P),
                                    start=(d == 0), stop=(d == DC - 1))
                mm.then_inc(S_["a_qk"], 1)
            if cls in ("q", "k"):
                if E_ == "vector":
                    if ft == 0 and p == 0:
                        self.W(eng, "f_d", b_fd)
                        self.W(eng, "o_mm", C["o_mm"])
                    eng.wait_ge(S_["a_qk"], b["a_qk"] + ft + 1)
                    if cls == "q":
                        dst = qt(ft * T + p * P)
                        dlo = bass.AP(dst.tensor, dst.offset, [[self.AW, 64], [1, P]])
                        dhi = bass.AP(dst.tensor, dst.offset + 64 * self.AW,
                                      [[self.AW, 64], [1, P]])
                    else:
                        kidx = ft - 32
                        self.W(eng, "a_kst", b["a_kst"] + 16 * (kidx - 1))
                        ko = self.ktx[:, :].offset + (ft % 2) * P
                        dlo = bass.AP(self.ktx[:, :].tensor, ko, [[2 * P, 64], [1, P]])
                        dhi = bass.AP(self.ktx[:, :].tensor, ko + 64 * 2 * P,
                                      [[2 * P, 64], [1, P]])
                    ps_ = pb[par]
                    plo, phi = ps_[0:64, :], ps_[64:128, :]
                    cs = bass.AP(self.cs_sb[:, :].tensor, self.cs_sb[:, :].offset + p * P,
                                 [[2 * T, 64], [1, P]])
                    sn = bass.AP(self.cs_sb[:, :].tensor,
                                 self.cs_sb[:, :].offset + T + p * P, [[2 * T, 64], [1, P]])
                    t0 = self.tsg[0:64, 0:P]
                    t1 = self.tsg[0:64, P:2 * P]
                    eng.tensor_tensor(t0, plo, cs, ALU.mult)
                    eng.tensor_tensor(t1, phi, sn, ALU.mult)
                    eng.tensor_tensor(dlo, t0, t1, ALU.subtract)
                    eng.tensor_tensor(t0, phi, cs, ALU.mult)
                    eng.tensor_tensor(t1, plo, sn, ALU.mult)
                    eng.tensor_tensor(dhi, t0, t1, ALU.add).then_inc(S_["a_rp"], 1)
                if cls == "k" and E_ == "scalar":
                    kidx = ft - 32
                    eng.wait_ge(S_["a_rp"], b["a_rp"] + ft + 1)
                    ko = self.ktx[:, :].offset + (ft % 2) * P
                    eng.dma_start(out=dsl(D["kown"].rearrange("h p t -> (h p) t"),
                                          T, kidx * 128, 128, p * P, P),
                                  in_=bass.AP(self.ktx[:, :].tensor, ko,
                                              [[2 * P, 128], [1, P]])).then_inc(S_["a_kst"], 16)
            else:
                vh = ft - 40
                if E_ == "scalar":
                    eng.wait_ge(S_["a_qk"], b["a_qk"] + ft + 1)
                    self.W(eng, "p_tp", b["p_tp"] + vh - 1)
                    eng.copy(self.vt[:, ts(ft % 2, P)], pb[par][:, :]).then_inc(S_["a_vv"], 1)
                if E_ == "tensor":
                    eng.wait_ge(S_["a_vv"], b["a_vv"] + vh + 1)
                    self.W(eng, "p_ve", b["p_ve"] + vh - 1)
                    mm = None
                    for jj in range(4):
                        mm = eng.transpose(self.pt[ft % 2][:, ts(jj, 128)],
                                           bass.AP(self.vt[:, :].tensor,
                                                   self.vt[:, :].offset + (ft % 2) * P + 128 * jj,
                                                   [[2 * P, 128], [1, 128]]),
                                           self.id_sb[:, :])
                    mm.then_inc(S_["p_tp"], 1)
                if E_ == "vector":
                    eng.wait_ge(S_["p_tp"], b["p_tp"] + vh + 1)
                    self.W(eng, "a_kst", b["a_kst"] + 16 * (8 + vh - 1))
                    eng.tensor_copy(self.vp[:, ts(ft % 2, P)],
                                    self.pt[ft % 2][:, 0:P]).then_inc(S_["p_ve"], 1)
                if E_ == "scalar":
                    eng.wait_ge(S_["p_ve"], b["p_ve"] + vh + 1)
                    dst = bass.AP(D["vown"].tensor,
                                  D["vown"].offset + (vh * T + 512 * p) * DH,
                                  [[DH, 128], [128 * DH, 4], [1, DH]])
                    src = bass.AP(self.vp[:, :].tensor,
                                  self.vp[:, :].offset + (ft % 2) * P,
                                  [[2 * P, 128], [128, 4], [1, 128]])
                    eng.dma_start(out=dst, in_=src).then_inc(S_["a_kst"], 16)
        C["f_lw"] = b["f_lw"] + 16 * 48
        C["a_qk"] = b["a_qk"] + 48
        C["a_rp"] = b["a_rp"] + 40
        C["a_vv"] = b["a_vv"] + 8
        C["a_kst"] = b["a_kst"] + 16 * 16
        C["p_tp"] = b["p_tp"] + 8
        C["p_ve"] = b["p_ve"] + 8

    def attn_chunk(self, E_, eng, j, xsrc, xdst):
        C, S_, D = self.C, self.S, self.D
        pb = self.pb
        NS = 8 if j == 0 else 16
        qt = lambda col: self.abf(0, P, col)
        ktf = lambda col: self.abf(90112, 128, col)
        vft = lambda col: self.abf(122880, 128, col)
        msk = lambda sl: self.abf(81920, P, sl * P)
        b = {k: C[k] for k in ("a_sc", "a_ex", "a_pv", "a_l", "a_no", "a_mk")}

        if E_ == "sync":
            self.W(eng, "a_ex", b["a_ex"])
            src = bass.AP(D["masks"].tensor, D["masks"].offset + 8 * j * 128 * P,
                          [[P, 128], [128 * P, 8], [1, P]])
            dst = bass.AP(self.arena[:, :].tensor, 81920 // 2,
                          [[self.AW, 128], [P, 8], [1, P]])
            eng.dma_start(out=dst, in_=src).then_inc(S_["a_ml"], 16)
        C["a_ml"] = C["a_ml"] + 16

        for hh in range(32):
            kvh = hh // 4
            qcol = hh * T + j * P
            sb = b["a_sc"] + hh * NS

            if E_ == "tensor":
                def sc_mm(s):
                    if hh == 0 and s == 0:
                        eng.wait_ge(S_["a_lkv"], C["a_lkv"])
                        eng.wait_ge(S_["a_ml"], C["a_ml"])
                    self.W(eng, "a_ex", b["a_ex"] + hh * NS + s - 1)
                    eng.matmul(pb[s % 2][:, :], ktf(kvh * 2048 + s * 128), qt(qcol),
                               start=True, stop=True).then_inc(S_["a_sc"], 1)

                def pv_mm(s):
                    eng.wait_ge(S_["a_ex"], b["a_ex"] + hh * NS + s + 1)
                    if s == 0:
                        self.W(eng, "a_no", b["a_no"] + hh)
                        self.W(eng, "a_l", b["a_l"] + 2 * hh)
                    eng.matmul(pb[2][:, :], vft((kvh * 16 + s) * 128),
                               self.ptb[:, ts(s % 2, P)], start=(s == 0), stop=(s == NS - 1))
                    mm = eng.matmul(pb[3][0:1, :], self.onesb_sb[:, :],
                                    self.ptb[:, ts(s % 2, P)],
                                    start=(s == 0), stop=(s == NS - 1))
                    if s == NS - 1:
                        mm.then_inc(S_["a_pv"], 1)

                for s in range(NS):
                    sc_mm(s)
                    if s >= 1:
                        pv_mm(s - 1)
                pv_mm(NS - 1)
                eng.wait_ge(S_["a_l"], b["a_l"] + 2 * hh + 2)
                eng.matmul(pb[4][:, :], self.onesf_sb[:, :], self.lrow[0:1, P:2 * P],
                           start=True, stop=True).then_inc(S_["a_pv"], 1)
            if E_ == "vector":
                for s in range(NS):
                    if j == 0 or s >= 8:
                        eng.wait_ge(S_["a_sc"], sb + s + 1)
                        mslot = s if j == 0 else s - 8
                        eng.tensor_tensor(pb[s % 2][:, :], pb[s % 2][:, :],
                                          msk(mslot), ALU.add).then_inc(S_["a_mk"], 1)
                eng.wait_ge(S_["a_pv"], b["a_pv"] + 2 * hh + 1)
                eng.tensor_copy(self.lrow[0:1, 0:P], pb[3][0:1, :])
                eng.reciprocal(self.lrow[0:1, P:2 * P],
                               self.lrow[0:1, 0:P]).then_inc(S_["a_l"], 2)
                eng.wait_ge(S_["a_pv"], b["a_pv"] + 2 * hh + 2)
                eng.tensor_copy(self.rlb[:, :], pb[4][:, :])
                eng.tensor_tensor(qt(qcol), pb[2][:, :], self.rlb[:, :],
                                  ALU.mult).then_inc(S_["a_no"], 1)
            if E_ == "scalar":
                for s in range(NS):
                    masked = (j == 0 or s >= 8)
                    eng.wait_ge(S_["a_sc"], sb + s + 1)
                    if masked:
                        mcnt = hh * 8 + (s + 1 if j == 0 else s - 8 + 1)
                        eng.wait_ge(S_["a_mk"], b["a_mk"] + mcnt)
                    eng.activation(self.ptb[:, ts(s % 2, P)], pb[s % 2][:, :],
                                   AF.Exp, scale=SCL).then_inc(S_["a_ex"], 1)

        C["a_sc"] = b["a_sc"] + 32 * NS
        C["a_mk"] = b["a_mk"] + 32 * 8
        C["a_ex"] = b["a_ex"] + 32 * NS
        C["a_pv"] = b["a_pv"] + 2 * 32
        C["a_l"] = b["a_l"] + 2 * 32
        C["a_no"] = b["a_no"] + 32

        b2 = {k: C[k] for k in ("o_mm", "o_res", "f_lw", "o_xr", "o_so")}
        wbo = [self.abf(65536, 4096), self.abf(73728, 4096)]
        for fo in range(DC):
            par = fo % 2
            wb = wbo[par]
            if E_ == "sync":
                if fo == 0 and j == 0:
                    self.W(eng, "a_rp", C["a_rp"])
                    self.W(eng, "f_d", C["f_d"])
                self.W(eng, "o_mm", b2["o_mm"] + fo - 1)
                eng.dma_start(out=wb, in_=D["gio"][10 * (fo // 4) + 6 + fo % 4]
                              ).then_inc(S_["f_lw"], 16)
                self.W(eng, "o_res", b2["o_res"] + fo - 1)
                eng.dma_start(out=self.xr[:, ts(par, P)],
                              in_=dsl(xsrc, T, fo * 128, 128, j * P, P)).then_inc(S_["o_xr"], 16)
            if E_ == "tensor":
                if fo == 0:
                    eng.wait_ge(S_["a_no"], C["a_no"])
                eng.wait_ge(S_["f_lw"], b2["f_lw"] + 16 * (fo + 1))
                self.W(eng, "o_res", b2["o_res"] + fo - 1)
                mm = None
                for e2 in range(DC):
                    mm = eng.matmul(pb[4 + par][:, :], wb[:, ts(e2, 128)],
                                    qt(e2 * T + j * P), start=(e2 == 0), stop=(e2 == DC - 1))
                mm.then_inc(S_["o_mm"], 1)
            if E_ == "vector":
                eng.wait_ge(S_["o_mm"], b2["o_mm"] + fo + 1)
                eng.wait_ge(S_["o_xr"], b2["o_xr"] + 16 * (fo + 1))
                self.W(eng, "o_so", b2["o_so"] + 16 * (fo - 1))
                eng.tensor_tensor(self.xo[:, ts(par, P)], pb[4 + par][:, :],
                                  self.xr[:, ts(par, P)], ALU.add).then_inc(S_["o_res"], 1)
            if E_ == "scalar":
                eng.wait_ge(S_["o_res"], b2["o_res"] + fo + 1)
                eng.dma_start(out=dsl(xdst, T, fo * 128, 128, j * P, P),
                              in_=self.xo[:, ts(par, P)]).then_inc(S_["o_so"], 16)
        C["o_mm"] = b2["o_mm"] + DC
        C["o_res"] = b2["o_res"] + DC
        C["f_lw"] = b2["f_lw"] + 16 * DC
        C["o_xr"] = b2["o_xr"] + 16 * DC
        C["o_so"] = b2["o_so"] + 16 * DC


def _core_tokens(c):
    par = c % 2
    blocks = (0, 3) if par == 0 else (1, 2)
    idx = np.concatenate([np.arange(P) + P * blocks[0], np.arange(P) + P * blocks[1]])
    return c // 2, blocks, idx


def _stage_inputs(L0_attn_x, L0_attn_w_in, L0_attn_w_out, L0_ffn_w1, L0_ffn_w2, L0_ffn_w3):
    x = np.asarray(L0_attn_x, np.float32)
    w_in = np.asarray(L0_attn_w_in, np.float32)
    w_out = np.asarray(L0_attn_w_out, np.float32)
    w1 = np.asarray(L0_ffn_w1, np.float32)
    w2 = np.asarray(L0_ffn_w2, np.float32)
    w3 = np.asarray(L0_ffn_w3, np.float32)
    w3p = np.zeros((E, HPAD), np.float32)
    w3p[:, :Hf] = w3
    inv = 1.0 / (10000.0 ** (np.arange(0, DH, 2, dtype=np.float32) / DH))
    ang = np.arange(S, dtype=np.float32)[:, None] * inv[None, :].astype(np.float32)
    cosT = np.cos(ang).astype(np.float32).T
    sinT = np.sin(ang).astype(np.float32).T
    idn = np.eye(128).astype(ml_dtypes.bfloat16)
    onesb = np.ones((128, 1), ml_dtypes.bfloat16)
    onesf = np.ones((1, 128), np.float32)

    ins = []
    for c in range(8):
        b, blocks, idx = _core_tokens(c)
        x_c = np.ascontiguousarray(x[b].reshape(S, E)[idx].T)
        t_ids = [8 * u + c for u in range(11)]

        def wsl(w):
            o = np.zeros((11, 128, E), np.float32)
            for u, t in enumerate(t_ids):
                if t * 128 < Hf:
                    o[u] = w[t * 128:(t + 1) * 128]
            return o

        s1, s2 = wsl(w1), wsl(w2)
        w12 = np.zeros((NU12, 128, E), np.float32)
        w12[0::2] = s1
        w12[1::2] = s2
        w3s = w3p[512 * c:512 * (c + 1)]
        w3u = np.zeros((NU3, 128, E), np.float32)
        for dt in range(4):
            for ch in range(4):
                w3u[4 * dt + ch, :, :W3C] = w3s[128 * dt:128 * (dt + 1),
                                                W3C * ch:W3C * (ch + 1)]
        wio = np.zeros((NUIO, 128, E), np.float32)
        for u in range(6):
            wio[u] = w_in[768 * c + 128 * u:768 * c + 128 * (u + 1)]
        for u in range(4):
            wio[6 + u] = w_out[512 * c + 128 * u:512 * c + 128 * (u + 1)]
        QB = [P * blocks[0], P * blocks[1]]
        m = np.zeros((16, 128, P), np.float32)
        for jj in range(2):
            for si in range(8):
                s_abs = si + 8 * jj
                kpos = 128 * s_abs + np.arange(128)[:, None]
                qpos = QB[jj] + np.arange(P)[None, :]
                m[8 * jj + si] = np.where(kpos <= qpos, 0.0, NEG)
        ins.append({
            "x_fm": x_c, "w12c": w12, "w3c": w3u, "wioc": wio,
            "cosq": np.ascontiguousarray(cosT[:, idx]),
            "sinq": np.ascontiguousarray(sinT[:, idx]),
            "masks": m.astype(ml_dtypes.bfloat16),
            "ident": idn, "onesb": onesb, "onesf": onesf,
            "nfb": np.tile(-_BOUNDS[None, :], (128, 1)).astype(np.float32),
        })
    return ins


def kernel(L0_attn_x, L0_attn_w_in, L0_attn_w_out, L0_ffn_w1, L0_ffn_w2, L0_ffn_w3,
           _sublayers=SUBLAYERS, _stop=None, _want_dbg=False):
    ins = _stage_inputs(L0_attn_x, L0_attn_w_in, L0_attn_w_out,
                        L0_ffn_w1, L0_ffn_w2, L0_ffn_w3)
    nc = KB(_sublayers, stop=_stop).build()
    res = run_bass_kernel_spmd(nc, ins, core_ids=list(range(8)))
    if _want_dbg:
        return res
    out = np.zeros((B, S, E), np.float32)
    for c in range(8):
        b, blocks, idx = _core_tokens(c)
        out[b].reshape(S, E)[idx] = np.asarray(res.results[c]["out_fm"], np.float32).T
    return out
